# version 03 (final)
# baseline (speedup 1.0000x reference)
import numpy as np
import concourse.bass as bass
import concourse.mybir as mybir
from concourse.bass_utils import run_bass_kernel_spmd

F32 = mybir.dt.float32
AF = mybir.ActivationFunctionType
ALU = mybir.AluOpType
AX = mybir.AxisListType

B, T, D = 4, 4096, 1024
NC = 8
TC = 2048
EPS = 1e-5
NEG = -1.0e30

f32 = np.float32



class Prog:

    def __init__(self):
        self.items = []

    def op(self, eng, fn):
        self.items.append((eng, fn, 1, "comp"))

    def dma(self, out, in_, eng="sync", **kw):
        kind = "store" if getattr(out.space, "name", "") == "DRAM" else "load"
        self.items.append(
            (eng, lambda e, o=out, i=in_, k=kw: e.dma_start(out=o, in_=i, **k), 16, kind))

    def finalize(self, nc, block, sems, rotate_at=30000, chunk=4000, serial=False):
        engs = {"sync": nc.sync, "gpsimd": nc.gpsimd, "scalar": nc.scalar,
                "vector": nc.vector, "tensor": nc.tensor}
        secs = {"sync": block.sync, "gpsimd": block.gpsimd, "scalar": block.scalar,
                "vector": block.vector, "tensor": block.tensor}
        total = sum(it[2] for it in self.items)
        assert total <= len(sems) * (rotate_at - chunk), \
            f"semaphore window capacity exceeded: {total} incs, {len(sems)} sems"
        groups = []
        for it in self.items:
            eng, fn, inc, kind = it
            if (not serial and kind == "comp" and eng == "tensor" and groups
                    and groups[-1][0] == eng and groups[-1][1] == kind
                    and groups[-1][3] + inc <= chunk):
                groups[-1][2].append(it)
                groups[-1][3] += inc
            else:
                groups.append([eng, kind, [it], inc])
        state = dict(si=0, count=0, prev=None)

        def emit_group(engname, kind, items, tot):
            eng = engs[engname]
            is_dma = kind != "comp"

            def body(_):
                if state["count"] + tot > rotate_at:
                    state["prev"] = (sems[state["si"]], state["count"])
                    state["si"] = (state["si"] + 1) % len(sems)
                    state["count"] = 0
                base = state["count"]
                sem = sems[state["si"]]
                first = True
                for _, fn, inc, _k in items:
                    if first or is_dma:
                        if state["prev"] is not None:
                            eng.wait_ge(state["prev"][0], state["prev"][1])
                        if base > 0:
                            eng.wait_ge(sem, base)
                    inst = fn(eng)
                    inst.then_inc(sem, inc)
                    state["count"] += inc
                    first = False
                state["prev"] = None
            secs[engname](body)

        for engname, kind, items, tot in groups:
            emit_group(engname, kind, items, tot)

        def fin(_):
            eng = engs["sync"]
            if state["prev"] is not None:
                eng.wait_ge(state["prev"][0], state["prev"][1])
            if state["count"] > 0:
                eng.wait_ge(sems[state["si"]], state["count"])
        secs["sync"](fin)


class K:

    def __init__(self, p):
        self.p = p

    def dma(self, out, in_, eng="sync"):
        self.p.dma(out, in_, eng=eng)

    def mm(self, out, lhsT, rhs, start=True, stop=True):
        self.p.op("tensor", lambda e, o=out, a=lhsT, b=rhs, s=start, t=stop:
                  e.matmul(o, a, b, start=s, stop=t))

    def tr(self, out, in_, ident):
        self.p.op("tensor", lambda e, o=out, a=in_, i=ident: e.transpose(o, a, i))

    def tt(self, out, a, b, op):
        self.p.op("vector", lambda e, o=out, x=a, y=b, op_=op: e.tensor_tensor(o, x, y, op_))

    def copy(self, out, in_):
        self.p.op("vector", lambda e, o=out, a=in_: e.tensor_copy(o, a))

    def ts(self, out, in0, s1, s2, op0, op1=ALU.bypass):
        self.p.op("vector", lambda e, o=out, a=in0, x=s1, y=s2, p0=op0, p1=op1:
                  e.tensor_scalar(o, a, x, y, p0, p1))

    def stt(self, out, in0, s, in1, op0, op1):
        self.p.op("vector", lambda e, o=out, a=in0, x=s, b=in1, p0=op0, p1=op1:
                  e.scalar_tensor_tensor(o, a, x, b, p0, p1))

    def act(self, out, in_, func, bias=0.0, scale=1.0):
        self.p.op("scalar", lambda e, o=out, a=in_, f=func, b=bias, s=scale:
                  e.activation(o, a, f, bias=b, scale=s))

    def rsum(self, out, in_, negate=False):
        self.p.op("vector", lambda e, o=out, a=in_, n=negate:
                  e.reduce_sum(o, a, axis=AX.X, negate=n))

    def rmax(self, out, in_, negate=False):
        self.p.op("vector", lambda e, o=out, a=in_, n=negate:
                  e.reduce_max(o, a, axis=AX.X, negate=n))

    def recip(self, out, in_):
        self.p.op("vector", lambda e, o=out, a=in_: e.reciprocal(o, a))

    def memset(self, ap, c):
        self.p.op("vector", lambda e, a=ap, v=c: e.memset(a, v))


class LNScratch:

    def __init__(self, nc):
        self.ctx = []
        def sb(name, shape):
            t = nc.sbuf_tensor(name, shape, F32)
            self.ctx.append(t)
            return t.__enter__()
        self.ones_col = sb("ln_ones_col", [128, 1])
        self.ones_row = sb("ln_ones_row", [1, 128])
        self.sq = sb("ln_sq", [128, 512])
        self.tmp = sb("ln_tmp", [128, 512])
        self.r_m = sb("ln_rm", [1, 512])
        self.r_s = sb("ln_rs", [1, 512])
        self.r_t = sb("ln_rt", [1, 512])
        self.r_r = sb("ln_rr", [1, 512])
        self.eps = sb("ln_eps", [1, 1])

    def init(self, k):
        k.memset(self.ones_col[:], 1.0)
        k.memset(self.ones_row[:], 1.0)
        k.memset(self.eps[:], EPS)


def emit_ln_aps(k, ln, in_aps, out_aps, w, psS, psQ, psM, psR, affine=None,
                dim=None):
    nd = len(in_aps)
    dim = dim if dim is not None else nd * 128
    for d in range(nd):
        k.mm(psS[0:1, 0:w], ln.ones_col[:], in_aps[d],
             start=(d == 0), stop=(d == nd - 1))
    for d in range(nd):
        k.act(ln.sq[:, 0:w], in_aps[d], AF.Square)
        k.mm(psQ[0:1, 0:w], ln.ones_col[:], ln.sq[:, 0:w],
             start=(d == 0), stop=(d == nd - 1))
    k.ts(ln.r_m[:, 0:w], psS[0:1, 0:w], 1.0 / dim, None, ALU.mult)
    k.ts(ln.r_s[:, 0:w], psQ[0:1, 0:w], 1.0 / dim, None, ALU.mult)
    k.tt(ln.r_t[:, 0:w], ln.r_m[:, 0:w], ln.r_m[:, 0:w], ALU.mult)
    k.tt(ln.r_s[:, 0:w], ln.r_s[:, 0:w], ln.r_t[:, 0:w], ALU.subtract)
    k.act(ln.r_t[:, 0:w], ln.r_s[:, 0:w], AF.Sqrt, bias=ln.eps[:])
    k.recip(ln.r_r[:, 0:w], ln.r_t[:, 0:w])
    k.mm(psM[:, 0:w], ln.ones_row[:], ln.r_m[:, 0:w])
    k.mm(psR[:, 0:w], ln.ones_row[:], ln.r_r[:, 0:w])
    for d in range(nd):
        k.tt(ln.tmp[:, 0:w], in_aps[d], psM[:, 0:w], ALU.subtract)
        k.tt(out_aps[d], ln.tmp[:, 0:w], psR[:, 0:w], ALU.mult)
        if affine is not None:
            g_sb, b_sb, col = affine
            k.ts(out_aps[d], out_aps[d],
                 g_sb[:, col + d:col + d + 1], b_sb[:, col + d:col + d + 1],
                 ALU.mult, ALU.add)


def emit_ln(k, ln, xt, c0, w, nd, psS, psQ, psM, psR, out=None, oc0=None,
            affine=None, dim=None):
    if out is None:
        out, oc0 = xt, c0
    emit_ln_aps(k, ln, [xt[d][:, c0:c0 + w] for d in range(nd)],
                [out[d][:, oc0:oc0 + w] for d in range(nd)], w,
                psS, psQ, psM, psR, affine=affine, dim=dim)


def finish(nc, p, block_sems):
    pass


_NC_CACHE = {}


def cached(name, builder):
    if name not in _NC_CACHE:
        _NC_CACHE[name] = builder()
    return _NC_CACHE[name]



L0A_GROUPS = [(0, 1), (1, 512), (513, 512), (1025, 512), (1537, 512)]


def build_l0a():
    nc = bass.Bass("TRN2", target_bir_lowering=False)
    dt = lambda n, s, k="ExternalInput": nc.dram_tensor(n, s, F32, kind=k).ap()
    xe_d = dt("xe", [D, TC + 1])
    mu_d = dt("xp_mu", [128, 8])
    w1_d = dt("xp_w1t", [D, 160])
    w2_d = dt("xp_w2g", [5, 32, D])
    xb_d = dt("xbias", [128, 40])
    rW_d = dt("rwt", [D, 512])
    wA_d = dt("wat", [D, 64])
    wB_d = dt("wbt", [64, 512])
    wb_d = dt("wb", [128, 4])
    kW_d = dt("kwt", [D, 512])
    vW_d = dt("vwt", [D, D])
    gA_d = dt("gat", [D, 64])
    gB_d = dt("gbt", [64, D])
    gb_d = dt("gb", [128, 8])
    r_o = dt("r_o", [512, TC], "ExternalOutput")
    w_o = dt("w_o", [512, TC], "ExternalOutput")
    k_o = dt("k_o", [512, TC], "ExternalOutput")
    v_o = dt("v_o", [D, TC], "ExternalOutput")
    g_o = dt("g_o", [D, TC], "ExternalOutput")

    ctx = []

    def sb(name, shape):
        t = nc.sbuf_tensor(name, shape, F32)
        ctx.append(t)
        return t.__enter__()

    def ps(name, shape=(128, 512)):
        t = nc.psum_tensor(name, list(shape), F32)
        ctx.append(t)
        return t.__enter__()

    xe = [sb(f"xe{d}", [128, TC + 1]) for d in range(8)]
    mu_sb = sb("mu_sb", [128, 8])
    xb_sb = sb("xb_sb", [128, 40])
    wb_sb = sb("wb_sb", [128, 4])
    gb_sb = sb("gb_sb", [128, 8])
    w1_sb = sb("w1_sb", [128, 8 * 160])
    w2_sb = sb("w2_sb", [32, 5 * 1024])
    wA_sb = sb("wA_sb", [128, 8 * 64])
    wB_sb = sb("wB_sb", [64, 512])
    gA_sb = sb("gA_sb", [128, 8 * 64])
    gB_sb = sb("gB_sb", [64, 1024])
    wstrip = sb("wstrip", [128, 1024])
    delta = [sb(f"delta{d}", [128, 512]) for d in range(8)]
    atile = [sb(f"atile{d}", [128, 512]) for d in range(8)]
    xl_sb = sb("xl_sb", [32, 5 * 512])
    t64 = sb("t64", [64, 512])
    tmp = sb("tmpA", [128, 512])
    tmp2 = sb("tmpB", [128, 512])
    otile = sb("otile", [128, 512])
    ln = LNScratch(nc)
    ctx.extend([])

    psS = ps("psS")
    psQ = ps("psQ")
    psM = ps("psM")
    psR = ps("psR")
    ps4 = ps("ps4")
    ps5 = ps("ps5")

    sems = [nc.semaphore(f"sm{i}").__enter__() for i in range(4)]
    block = nc.Block().__enter__()

    p = Prog()
    k = K(p)
    ln.init(k)
    for d in range(8):
        k.dma(xe[d][:], xe_d[d * 128:(d + 1) * 128, :])
    k.dma(mu_sb[:], mu_d)
    k.dma(xb_sb[:], xb_d)
    k.dma(wb_sb[:], wb_d)
    k.dma(gb_sb[:], gb_d)
    k.dma(w1_sb[:].rearrange("p (a o) -> p a o", a=8), w1_d.rearrange("(a p) o -> p a o", p=128))
    k.dma(w2_sb[:].rearrange("p (n o) -> p n o", n=5), w2_d.rearrange("n r h -> r n h"))
    k.dma(wA_sb[:].rearrange("p (a o) -> p a o", a=8), wA_d.rearrange("(a p) o -> p a o", p=128))
    k.dma(wB_sb[:], wB_d)
    k.dma(gA_sb[:].rearrange("p (a o) -> p a o", a=8), gA_d.rearrange("(a p) o -> p a o", p=128))
    k.dma(gB_sb[:], gB_d)

    for (c0, w) in L0A_GROUPS:
        emit_ln(k, ln, xe, c0, w, 8, psS, psQ, psM, psR)

    wv = lambda d: wstrip[:, 128 * d:128 * d + 128]
    w1v = lambda d, a, bcol: w1_sb[:, 160 * d + a:160 * d + bcol]
    wAv = lambda d: wA_sb[:, 64 * d:64 * d + 64]
    gAv = lambda d: gA_sb[:, 64 * d:64 * d + 64]

    for (c0, w) in L0A_GROUPS[1:]:
        t0 = c0 - 1
        for d in range(8):
            k.tt(delta[d][:], xe[d][:, c0 - 1:c0 - 1 + 512], xe[d][:, c0:c0 + 512], ALU.subtract)
        for d in range(8):
            k.stt(atile[d][:], delta[d][:], mu_sb[:, d:d + 1], xe[d][:, c0:c0 + 512], ALU.mult, ALU.add)
        for n in range(5):
            for d in range(8):
                k.mm(ps5[0:32, 0:512], w1v(d, 32 * n, 32 * n + 32), atile[d][:],
                     start=(d == 0), stop=(d == 7))
            k.act(xl_sb[:, 512 * n:512 * n + 512], ps5[0:32, 0:512], AF.Tanh)
        for n, br in enumerate(["r", "w", "k", "v", "g"]):
            for d in range(8):
                k.mm(ps4[:, 0:512], w2_sb[:, 1024 * n + 128 * d:1024 * n + 128 * d + 128],
                     xl_sb[:, 512 * n:512 * n + 512])
                k.ts(tmp[:], ps4[:, 0:512], xb_sb[:, 8 * n + d:8 * n + d + 1], None, ALU.add)
                k.tt(tmp2[:], delta[d][:], tmp[:], ALU.mult)
                k.tt(atile[d][:], xe[d][:, c0:c0 + 512], tmp2[:], ALU.add)
            if br in ("r", "k"):
                W = rW_d if br == "r" else kW_d
                O = r_o if br == "r" else k_o
                for oc in range(4):
                    k.dma(wstrip[:].rearrange("p (a o) -> p a o", a=8), W[:, 128 * oc:128 * oc + 128].rearrange("(a p) o -> p a o", p=128))
                    for d in range(8):
                        k.mm(ps4[:, 0:512], wv(d), atile[d][:], start=(d == 0), stop=(d == 7))
                    k.copy(otile[:], ps4[:, 0:512])
                    k.dma(O[128 * oc:128 * oc + 128, t0:t0 + 512], otile[:])
            elif br == "v":
                for oc in range(8):
                    k.dma(wstrip[:].rearrange("p (a o) -> p a o", a=8), vW_d[:, 128 * oc:128 * oc + 128].rearrange("(a p) o -> p a o", p=128))
                    for d in range(8):
                        k.mm(ps4[:, 0:512], wv(d), atile[d][:], start=(d == 0), stop=(d == 7))
                    k.copy(otile[:], ps4[:, 0:512])
                    k.dma(v_o[128 * oc:128 * oc + 128, t0:t0 + 512], otile[:])
            elif br == "w":
                for d in range(8):
                    k.mm(ps5[0:64, 0:512], wAv(d), atile[d][:], start=(d == 0), stop=(d == 7))
                k.act(t64[:], ps5[0:64, 0:512], AF.Tanh)
                for oc in range(4):
                    k.mm(ps4[:, 0:512], wB_sb[:, 128 * oc:128 * oc + 128], t64[:])
                    k.ts(otile[:], ps4[:, 0:512], wb_sb[:, oc:oc + 1], None, ALU.add)
                    k.dma(w_o[128 * oc:128 * oc + 128, t0:t0 + 512], otile[:])
            else:
                for d in range(8):
                    k.mm(ps5[0:64, 0:512], gAv(d), atile[d][:], start=(d == 0), stop=(d == 7))
                k.act(t64[:], ps5[0:64, 0:512], AF.Tanh)
                for oc in range(8):
                    k.mm(ps4[:, 0:512], gB_sb[:, 128 * oc:128 * oc + 128], t64[:])
                    k.ts(tmp[:], ps4[:, 0:512], gb_sb[:, oc:oc + 1], None, ALU.add)
                    k.act(otile[:], tmp[:], AF.Silu)
                    k.dma(g_o[128 * oc:128 * oc + 128, t0:t0 + 512], otile[:])

    p.finalize(nc, block, sems)
    block.__exit__(None, None, None)
    for s in reversed(sems):
        pass
    return nc


def prep_l0a(inp, st):
    xs = inp["L0_attn_hidden_states"]
    shared = dict(
        xp_mu=np.ascontiguousarray(inp["L0_attn_xp_mu"].reshape(8, 128).T),
        xp_w1t=np.ascontiguousarray(inp["L0_attn_xp_W1"].T),
        xp_w2g=np.ascontiguousarray(inp["L0_attn_xp_W2"].reshape(D, 5, 32).transpose(1, 2, 0)),
        xbias=np.ascontiguousarray(inp["L0_attn_x_bias"].reshape(5, 8, 128).transpose(2, 0, 1).reshape(128, 40)),
        rwt=np.ascontiguousarray(inp["L0_attn_r_W"].T),
        wat=np.ascontiguousarray(inp["L0_attn_w_A"].T),
        wbt=np.ascontiguousarray(inp["L0_attn_w_B"].T),
        wb=np.ascontiguousarray(inp["L0_attn_w_b"].reshape(4, 128).T),
        kwt=np.ascontiguousarray(inp["L0_attn_k_W"].T),
        vwt=np.ascontiguousarray(inp["L0_attn_v_W"].T),
        gat=np.ascontiguousarray(inp["L0_attn_g_A"].T),
        gbt=np.ascontiguousarray(inp["L0_attn_g_B"].T),
        gb=np.ascontiguousarray(inp["L0_attn_g_b"].reshape(8, 128).T),
    )
    in_maps = []
    for c in range(NC):
        b, h = c // 2, c % 2
        xf = np.ascontiguousarray(xs[b, h * TC:(h + 1) * TC, :].T)
        halo = np.zeros((D, 1), f32) if h == 0 else \
            np.ascontiguousarray(xs[b, TC - 1:TC, :].T)
        m = dict(shared)
        m["xe"] = np.ascontiguousarray(np.concatenate([halo, xf], axis=1))
        in_maps.append(m)
        st.setdefault("x_fm", {})[c] = np.ascontiguousarray(xs[b, h * TC:(h + 1) * TC, :].T)
    return in_maps


def run_l0a(inp, st):
    nc = cached("l0a", build_l0a)
    res = run_bass_kernel_spmd(nc, prep_l0a(inp, st), core_ids=list(range(NC)))
    for key in ["r_o", "w_o", "k_o", "v_o", "g_o"]:
        st[key] = [res.results[c][key] for c in range(NC)]
    return res




CHUNK = 32
NCHUNK = T // CHUNK


def build_rwkv():
    nc = bass.Bass("TRN2", target_bir_lowering=False)
    dt = lambda n, s, k="ExternalInput": nc.dram_tensor(n, s, F32, kind=k).ap()
    r_d = dt("r_u", [2, 128, T])
    w_d = dt("w_u", [2, 128, T])
    k_d = dt("k_u", [2, 128, T])
    v_d = dt("v_u", [2, 256, T])
    bon_d = dt("bon", [128, 2])
    id_d = dt("ident", [128, 128])
    um_d = dt("umat", [32, 32])
    mst_d = dt("mst", [32, 32])
    mdg_d = dt("mdg", [32, 32])
    o_d = dt("o_u", [2, 256, T], "ExternalOutput")

    ctx = []

    def sb(name, shape):
        t = nc.sbuf_tensor(name, shape, F32)
        ctx.append(t)
        return t.__enter__()

    def ps(name, shape):
        t = nc.psum_tensor(name, list(shape), F32)
        ctx.append(t)
        return t.__enter__()

    o0 = sb("o0", [128, T])
    o1 = sb("o1", [128, T])
    h_sb = sb("h_sb", [128, 256])
    ident = sb("identt", [128, 128])
    um = sb("um", [32, 32])
    mst = sb("mstt", [32, 32])
    mdg = sb("mdgt", [32, 32])
    bon = sb("bont", [128, 2])
    w_c = sb("w_c", [128, 32])
    r_c = sb("r_c", [128, 32])
    k_c = sb("k_c", [128, 32])
    v0_c = sb("v0_c", [128, 32])
    v1_c = sb("v1_c", [128, 32])
    wb_t = sb("wb_t", [128, 32])
    wbT = sb("wbT", [32, 128])
    ci_sb = sb("ci_sb", [128, 32])
    tmp32 = sb("tmp32", [128, 32])
    eq_t = sb("eq_t", [128, 32])
    qt = sb("qt", [128, 32])
    ek_t = sb("ek_t", [128, 32])
    kt = sb("kt", [128, 32])
    ed_t = sb("ed_t", [128, 32])
    kd = sb("kd", [128, 32])
    ru = sb("ru", [128, 32])
    A_sb = sb("A_sb", [32, 32])
    t33 = sb("t33", [32, 32])
    vT = sb("vT", [32, 256])
    kdT = sb("kdT", [32, 128])
    et = sb("et", [128, 1])

    psT = ps("psT", (32, 128))
    psC = ps("psC", (128, 32))
    psA = ps("psA", (32, 32))
    psD = ps("psD", (32, 32))
    psO0 = ps("psO0", (128, 32))
    psO1 = ps("psO1", (128, 32))
    psH = ps("psH", (128, 256))

    sems = [nc.semaphore(f"sm{i}").__enter__() for i in range(6)]
    block = nc.Block().__enter__()
    p = Prog()
    k = K(p)
    k.dma(ident[:], id_d)
    k.dma(um[:], um_d)
    k.dma(mst[:], mst_d)
    k.dma(mdg[:], mdg_d)
    k.dma(bon[:], bon_d)

    for u in range(2):
        k.memset(h_sb[:], 0.0)
        for c in range(NCHUNK):
            t0 = CHUNK * c
            sl = slice(t0, t0 + 32)
            k.dma(w_c[:], w_d[u, :, sl])
            k.dma(r_c[:], r_d[u, :, sl])
            k.dma(k_c[:], k_d[u, :, sl])
            k.dma(v0_c[:], v_d[u, 0:128, sl])
            k.dma(v1_c[:], v_d[u, 128:256, sl])
            k.act(wb_t[:], w_c[:], AF.Exp)
            k.ts(wb_t[:], wb_t[:], -1.0, None, ALU.mult)
            k.tr(psT[0:32, :], wb_t[:], ident[:])
            k.copy(wbT[:], psT[0:32, :])
            k.mm(psC[:, 0:32], wbT[:], um[:])
            k.copy(ci_sb[:], psC[:, 0:32])
            k.tt(tmp32[:], ci_sb[:], wb_t[:], ALU.subtract)
            k.act(eq_t[:], tmp32[:], AF.Exp)
            k.tt(qt[:], r_c[:], eq_t[:], ALU.mult)
            k.act(ek_t[:], ci_sb[:], AF.Exp, scale=-1.0)
            k.tt(kt[:], k_c[:], ek_t[:], ALU.mult)
            k.act(ed_t[:], ci_sb[:], AF.Exp, scale=-1.0, bias=ci_sb[:, 31:32])
            k.tt(kd[:], k_c[:], ed_t[:], ALU.mult)
            k.ts(ru[:], r_c[:], bon[:, u:u + 1], None, ALU.mult)
            k.mm(psA[:, :], kt[:], qt[:])
            k.mm(psD[:, :], k_c[:], ru[:])
            k.tt(A_sb[:], psA[:, :], mst[:], ALU.mult)
            k.tt(t33[:], psD[:, :], mdg[:], ALU.mult)
            k.tt(A_sb[:], A_sb[:], t33[:], ALU.add)
            k.tr(psT[0:32, :], v0_c[:], ident[:])
            k.copy(vT[:, 0:128], psT[0:32, :])
            k.tr(psT[0:32, :], v1_c[:], ident[:])
            k.copy(vT[:, 128:256], psT[0:32, :])
            for vh, pso in ((0, psO0), (1, psO1)):
                k.mm(pso[:, 0:32], h_sb[:, 128 * vh:128 * vh + 128], qt[:],
                     start=True, stop=False)
                k.mm(pso[:, 0:32], vT[:, 128 * vh:128 * vh + 128], A_sb[:],
                     start=False, stop=True)
            k.copy(o0[:, sl], psO0[:, 0:32])
            k.copy(o1[:, sl], psO1[:, 0:32])
            k.tr(psT[0:32, :], kd[:], ident[:])
            k.copy(kdT[:], psT[0:32, :])
            k.mm(psH[:, :], kdT[:], vT[:])
            k.act(et[:], ci_sb[:, 31:32], AF.Exp)
            k.stt(h_sb[:], h_sb[:], et[:], psH[:, :], ALU.mult, ALU.add)
        k.dma(o_d[u, 0:128, :], o0[:])
        k.dma(o_d[u, 128:256, :], o1[:])

    p.finalize(nc, block, sems)
    block.__exit__(None, None, None)
    return nc


def prep_rwkv(inp, st):
    Rf, Wf, Kf, Vf = {}, {}, {}, {}
    for b in range(B):
        Rf[b] = np.concatenate([st["r_o"][2 * b], st["r_o"][2 * b + 1]], axis=1)
        Wf[b] = np.concatenate([st["w_o"][2 * b], st["w_o"][2 * b + 1]], axis=1)
        Kf[b] = np.concatenate([st["k_o"][2 * b], st["k_o"][2 * b + 1]], axis=1)
        Vf[b] = np.concatenate([st["v_o"][2 * b], st["v_o"][2 * b + 1]], axis=1)
    bonus = inp["L0_attn_bonus"]
    ident = np.eye(128, dtype=f32)
    sidx = np.arange(32)
    um = (sidx[:, None] <= sidx[None, :]).astype(f32)
    mst = (sidx[:, None] < sidx[None, :]).astype(f32)
    mdg = np.eye(32, dtype=f32)
    in_maps = []
    for c in range(NC):
        b = c // 2
        h0, h1 = 2 * (c % 2), 2 * (c % 2) + 1
        m = dict(
            r_u=np.ascontiguousarray(np.stack([Rf[b][128 * h0:128 * h0 + 128],
                                               Rf[b][128 * h1:128 * h1 + 128]])),
            w_u=np.ascontiguousarray(np.stack([Wf[b][128 * h0:128 * h0 + 128],
                                               Wf[b][128 * h1:128 * h1 + 128]])),
            k_u=np.ascontiguousarray(np.stack([Kf[b][128 * h0:128 * h0 + 128],
                                               Kf[b][128 * h1:128 * h1 + 128]])),
            v_u=np.ascontiguousarray(np.stack([Vf[b][256 * h0:256 * h0 + 256],
                                               Vf[b][256 * h1:256 * h1 + 256]])),
            bon=np.ascontiguousarray(np.stack([bonus[h0], bonus[h1]], axis=1)),
            ident=ident, umat=um, mst=mst, mdg=mdg,
        )
        in_maps.append(m)
    return in_maps


def run_rwkv(inp, st):
    nc = cached("rwkv", build_rwkv)
    res = run_bass_kernel_spmd(nc, prep_rwkv(inp, st), core_ids=list(range(NC)))
    Of = {b: np.zeros((D, T), f32) for b in range(B)}
    for c in range(NC):
        b = c // 2
        h0, h1 = 2 * (c % 2), 2 * (c % 2) + 1
        o = res.results[c]["o_u"]
        Of[b][256 * h0:256 * h0 + 256] = o[0]
        Of[b][256 * h1:256 * h1 + 256] = o[1]
    st["o_fm"] = [np.ascontiguousarray(Of[c // 2][:, (c % 2) * TC:(c % 2 + 1) * TC])
                  for c in range(NC)]
    return res



def build_l0b():
    nc = bass.Bass("TRN2", target_bir_lowering=False)
    dt = lambda n, s, k="ExternalInput": nc.dram_tensor(n, s, F32, kind=k).ap()
    o_dd = dt("o_fm", [D, TC])
    gs_d = dt("gs_fm", [D, TC])
    x_d = dt("x_fm", [D, TC])
    gnw_d = dt("gnw", [128, 8])
    gnb_d = dt("gnb", [128, 8])
    oW_d = dt("owt", [D, D])
    xo_d = dt("xo", [D, TC], "ExternalOutput")

    ctx = []

    def sb(name, shape):
        t = nc.sbuf_tensor(name, shape, F32)
        ctx.append(t)
        return t.__enter__()

    def ps(name, shape=(128, 512)):
        t = nc.psum_tensor(name, list(shape), F32)
        ctx.append(t)
        return t.__enter__()

    ot = [sb(f"ot{d}", [128, TC]) for d in range(8)]
    xt = [sb(f"xt{d}", [128, TC]) for d in range(8)]
    gst = [sb(f"gst{d}", [128, 512]) for d in range(8)]
    yt = [sb(f"yt{d}", [128, 512]) for d in range(8)]
    gnw = sb("gnw_sb", [128, 8])
    gnb = sb("gnb_sb", [128, 8])
    wstrip = sb("wstripb", [128, 1024])
    ln = LNScratch(nc)
    psS, psQ, psM, psR, ps4 = ps("psS"), ps("psQ"), ps("psM"), ps("psR"), ps("ps4")

    sems = [nc.semaphore(f"sm{i}").__enter__() for i in range(4)]
    block = nc.Block().__enter__()
    p = Prog()
    k = K(p)
    ln.init(k)
    for d in range(8):
        k.dma(ot[d][:], o_dd[128 * d:128 * d + 128, :])
        k.dma(xt[d][:], x_d[128 * d:128 * d + 128, :])
    k.dma(gnw[:], gnw_d)
    k.dma(gnb[:], gnb_d)

    for g in range(4):
        c0 = 512 * g
        emit_ln(k, ln, ot, c0, 512, 8, psS, psQ, psM, psR, affine=(gnw, gnb, 0))
        for d in range(8):
            k.dma(gst[d][:], gs_d[128 * d:128 * d + 128, c0:c0 + 512])
        for d in range(8):
            k.tt(ot[d][:, c0:c0 + 512], ot[d][:, c0:c0 + 512], gst[d][:], ALU.mult)
        for oc in range(8):
            k.dma(wstrip[:].rearrange("p (a o) -> p a o", a=8),
                  oW_d[:, 128 * oc:128 * oc + 128].rearrange("(a p) o -> p a o", p=128))
            for d in range(8):
                k.mm(ps4[:, 0:512], wstrip[:, 128 * d:128 * d + 128],
                     ot[d][:, c0:c0 + 512], start=(d == 0), stop=(d == 7))
            k.copy(yt[oc][:], ps4[:, 0:512])
        emit_ln(k, ln, yt, 0, 512, 8, psS, psQ, psM, psR)
        for d in range(8):
            k.tt(yt[d][:], yt[d][:], xt[d][:, c0:c0 + 512], ALU.add)
            k.dma(xo_d[128 * d:128 * d + 128, c0:c0 + 512], yt[d][:])

    p.finalize(nc, block, sems)
    block.__exit__(None, None, None)
    return nc


def run_l0b(inp, st):
    nc = cached("l0b", build_l0b)
    shared = dict(
        gnw=np.ascontiguousarray(inp["L0_attn_gn_w"].reshape(8, 128).T),
        gnb=np.ascontiguousarray(inp["L0_attn_gn_b"].reshape(8, 128).T),
        owt=np.ascontiguousarray(inp["L0_attn_o_W"].T),
    )
    in_maps = []
    for c in range(NC):
        m = dict(shared)
        m["o_fm"] = st["o_fm"][c]
        m["gs_fm"] = st["g_o"][c]
        m["x_fm"] = st["x_fm"][c]
        in_maps.append(m)
    res = run_bass_kernel_spmd(nc, in_maps, core_ids=list(range(NC)))
    st["x_fm"] = {c: res.results[c]["xo"] for c in range(NC)}
    return res




def build_l0ffn():
    nc = bass.Bass("TRN2", target_bir_lowering=False)
    dt = lambda n, s, k="ExternalInput": nc.dram_tensor(n, s, F32, kind=k).ap()
    x_d = dt("x_fm", [D, TC])
    w1_d = dt("w1t", [D, 4096])
    w2_d = dt("w2t", [D, 4096])
    h1_d = dt("h1t", [4096, 4096])
    h2_d = dt("h2t", [4096, 4096])
    w3_d = dt("w3t", [4096, D])
    s1_d = dt("s1", [128, 32])
    s2_d = dt("s2", [128, 32])
    hs1_d = dt("hs1", [128, 32])
    hs2_d = dt("hs2", [128, 32])
    s3_d = dt("s3", [128, 8])
    xo_d = dt("xo", [D, TC], "ExternalOutput")
    xn_s = dt("xn_scr", [D, TC], "Internal")
    h_s = dt("h_scr", [4096, TC], "Internal")
    h1_s = dt("h1_scr", [4096, TC], "Internal")
    h2_s = dt("h2_scr", [4096, TC], "Internal")

    ctx = []

    def sb(name, shape):
        t = nc.sbuf_tensor(name, shape, F32)
        ctx.append(t)
        return t.__enter__()

    def ps(name, shape=(128, 512)):
        t = nc.psum_tensor(name, list(shape), F32)
        ctx.append(t)
        return t.__enter__()

    ht = [sb(f"hht{d}", [128, 512]) for d in range(32)]
    xg = [sb(f"xg{d}", [128, 512]) for d in range(8)]
    yt = [sb(f"yyt{d}", [128, 512]) for d in range(8)]
    wstrip = sb("wstripf", [128, 4096])
    wstrip2 = sb("wstripf2", [128, 1024])
    htile = sb("htile", [128, 512])
    sc = {n: sb(n + "_sb", [128, 32]) for n in ["s1", "s2", "hs1", "hs2"]}
    s3 = sb("s3_sb", [128, 8])
    ln = LNScratch(nc)
    psS, psQ, psM, psR, psA, psB = (ps(n) for n in ["psS", "psQ", "psM", "psR", "psA", "psB"])

    sems = [nc.semaphore(f"sm{i}").__enter__() for i in range(8)]
    block = nc.Block().__enter__()
    p = Prog()
    k = K(p)
    ln.init(k)
    for n, d_ in [("s1", s1_d), ("s2", s2_d), ("hs1", hs1_d), ("hs2", hs2_d)]:
        k.dma(sc[n][:], d_)
    k.dma(s3[:], s3_d)

    def loadstrip(dst, col0, W, oc, nrow):
        for q in range(nrow // 1024):
            k.dma(dst[:, col0 + 1024 * q:col0 + 1024 * q + 1024].rearrange("p (a o) -> p a o", a=8),
                  W[1024 * q:1024 * q + 1024, 128 * oc:128 * oc + 128]
                  .rearrange("(a p) o -> p a o", p=128))

    for g in range(4):
        c0 = 512 * g
        for d in range(8):
            k.dma(xg[d][:], x_d[128 * d:128 * d + 128, c0:c0 + 512])
        emit_ln(k, ln, xg, 0, 512, 8, psS, psQ, psM, psR)
        for d in range(8):
            k.dma(xn_s[128 * d:128 * d + 128, c0:c0 + 512], xg[d][:])

    for g in range(4):
        c0 = 512 * g
        for d in range(8):
            k.dma(xg[d][:], xn_s[128 * d:128 * d + 128, c0:c0 + 512])
        for oc in range(32):
            loadstrip(wstrip, 0, w1_d, oc, 1024)
            loadstrip(wstrip2, 0, w2_d, oc, 1024)
            for d in range(8):
                k.mm(psA[:, 0:512], wstrip[:, 128 * d:128 * d + 128], xg[d][:],
                     start=(d == 0), stop=(d == 7))
            for d in range(8):
                k.mm(psB[:, 0:512], wstrip2[:, 128 * d:128 * d + 128], xg[d][:],
                     start=(d == 0), stop=(d == 7))
            k.act(htile[:], psA[:, 0:512], AF.Silu, scale=sc["s1"][:, oc:oc + 1])
            k.stt(htile[:], psB[:, 0:512], sc["s2"][:, oc:oc + 1], htile[:],
                  ALU.mult, ALU.mult)
            k.dma(h_s[128 * oc:128 * oc + 128, c0:c0 + 512], htile[:])

    for (W, hs, src_s, dst_s) in [(h1_d, "hs1", h_s, h1_s), (h2_d, "hs2", h1_s, h2_s)]:
        for g in range(4):
            c0 = 512 * g
            for d in range(32):
                k.dma(ht[d][:], src_s[128 * d:128 * d + 128, c0:c0 + 512])
            for oc in range(32):
                loadstrip(wstrip, 0, W, oc, 4096)
                for d in range(32):
                    k.mm(psA[:, 0:512], wstrip[:, 128 * d:128 * d + 128], ht[d][:],
                         start=(d == 0), stop=(d == 31))
                k.act(htile[:], psA[:, 0:512], AF.Gelu, scale=sc[hs][:, oc:oc + 1])
                k.dma(dst_s[128 * oc:128 * oc + 128, c0:c0 + 512], htile[:])

    for g in range(4):
        c0 = 512 * g
        for d in range(32):
            k.dma(ht[d][:], h2_s[128 * d:128 * d + 128, c0:c0 + 512])
        for d in range(8):
            k.dma(xg[d][:], x_d[128 * d:128 * d + 128, c0:c0 + 512])
        for oc in range(8):
            loadstrip(wstrip, 0, w3_d, oc, 4096)
            for d in range(32):
                k.mm(psA[:, 0:512], wstrip[:, 128 * d:128 * d + 128], ht[d][:],
                     start=(d == 0), stop=(d == 31))
            k.ts(yt[oc][:], psA[:, 0:512], s3[:, oc:oc + 1], None, ALU.mult)
        emit_ln(k, ln, yt, 0, 512, 8, psS, psQ, psM, psR)
        for d in range(8):
            k.tt(yt[d][:], yt[d][:], xg[d][:], ALU.add)
            k.dma(xo_d[128 * d:128 * d + 128, c0:c0 + 512], yt[d][:])

    p.finalize(nc, block, sems)
    block.__exit__(None, None, None)
    return nc


def run_l0ffn(inp, st):
    nc = cached("l0ffn", build_l0ffn)
    shared = dict(
        w1t=np.ascontiguousarray(inp["L0_ffn_w1_w"].T),
        w2t=np.ascontiguousarray(inp["L0_ffn_w2_w"].T),
        h1t=np.ascontiguousarray(inp["L0_ffn_h1_w"].T),
        h2t=np.ascontiguousarray(inp["L0_ffn_h2_w"].T),
        w3t=np.ascontiguousarray(inp["L0_ffn_w3_w"].T),
        s1=np.ascontiguousarray(inp["L0_ffn_w1_s"].reshape(32, 128).T),
        s2=np.ascontiguousarray(inp["L0_ffn_w2_s"].reshape(32, 128).T),
        hs1=np.ascontiguousarray(inp["L0_ffn_h1_s"].reshape(32, 128).T),
        hs2=np.ascontiguousarray(inp["L0_ffn_h2_s"].reshape(32, 128).T),
        s3=np.ascontiguousarray(inp["L0_ffn_w3_s"].reshape(8, 128).T),
    )
    in_maps = []
    for c in range(NC):
        m = dict(shared)
        m["x_fm"] = st["x_fm"][c]
        in_maps.append(m)
    res = run_bass_kernel_spmd(nc, in_maps, core_ids=list(range(NC)))
    st["x_fm"] = {c: res.results[c]["xo"] for c in range(NC)}
    return res



def build_l1a():
    nc = bass.Bass("TRN2", target_bir_lowering=False)
    dt = lambda n, s, k="ExternalInput": nc.dram_tensor(n, s, F32, kind=k).ap()
    x_d = dt("x_fm", [D, TC])
    br_d = dt("brt", [D, 512])
    bi_d = dt("bit", [D, 512])
    dd_d = dt("dtt", [D, D])
    bur_d = dt("bur", [512, TC], "ExternalOutput")
    bui_d = dt("bui", [512, TC], "ExternalOutput")
    du_d = dt("du", [D, TC], "ExternalOutput")

    ctx = []

    def sb(name, shape):
        t = nc.sbuf_tensor(name, shape, F32)
        ctx.append(t)
        return t.__enter__()

    def ps(name, shape=(128, 512)):
        t = nc.psum_tensor(name, list(shape), F32)
        ctx.append(t)
        return t.__enter__()

    xt = [sb(f"xt{d}", [128, TC]) for d in range(8)]
    wstrip = sb("wstripa", [128, 1024])
    otile = sb("otileA", [128, 512])
    ln = LNScratch(nc)
    psS, psQ, psM, psR, ps4 = (ps(n) for n in ["psS", "psQ", "psM", "psR", "ps4"])
    sems = [nc.semaphore(f"sm{i}").__enter__() for i in range(4)]
    block = nc.Block().__enter__()
    p = Prog()
    k = K(p)
    ln.init(k)
    for d in range(8):
        k.dma(xt[d][:], x_d[128 * d:128 * d + 128, :])
    for g in range(4):
        c0 = 512 * g
        emit_ln(k, ln, xt, c0, 512, 8, psS, psQ, psM, psR)
        for (W, O, noc) in [(br_d, bur_d, 4), (bi_d, bui_d, 4), (dd_d, du_d, 8)]:
            for oc in range(noc):
                k.dma(wstrip[:].rearrange("p (a o) -> p a o", a=8),
                      W[:, 128 * oc:128 * oc + 128].rearrange("(a p) o -> p a o", p=128))
                for d in range(8):
                    k.mm(ps4[:, 0:512], wstrip[:, 128 * d:128 * d + 128],
                         xt[d][:, c0:c0 + 512], start=(d == 0), stop=(d == 7))
                k.copy(otile[:], ps4[:, 0:512])
                k.dma(O[128 * oc:128 * oc + 128, c0:c0 + 512], otile[:])
    p.finalize(nc, block, sems)
    block.__exit__(None, None, None)
    return nc


def run_l1a(inp, st):
    nc = cached("l1a", build_l1a)
    shared = dict(
        brt=np.ascontiguousarray(inp["L1_attn_B_re"].T),
        bit=np.ascontiguousarray(inp["L1_attn_B_im"].T),
        dtt=np.ascontiguousarray(inp["L1_attn_D"].T),
    )
    in_maps = []
    for c in range(NC):
        m = dict(shared)
        m["x_fm"] = st["x_fm"][c]
        in_maps.append(m)
    res = run_bass_kernel_spmd(nc, in_maps, core_ids=list(range(NC)))
    for key in ["bur", "bui", "du"]:
        st[key] = [res.results[c][key] for c in range(NC)]
    return res



LRU_PI = float(np.pi)


def build_lru():
    nc = bass.Bass("TRN2", target_bir_lowering=False)
    dt = lambda n, s, k="ExternalInput": nc.dram_tensor(n, s, F32, kind=k).ap()
    br_d = dt("bur_u", [256, T])
    bi_d = dt("bui_u", [256, T])
    nu_d = dt("nu", [128, 2])
    th_d = dt("th", [128, 2])
    gm_d = dt("gm", [128, 2])
    xr_d = dt("xre_u", [256, T], "ExternalOutput")
    xi_d = dt("xim_u", [256, T], "ExternalOutput")

    ctx = []

    def sb(name, shape):
        t = nc.sbuf_tensor(name, shape, F32)
        ctx.append(t)
        return t.__enter__()

    cr = [sb(f"cr{g}", [128, T]) for g in range(2)]
    ci = [sb(f"ci{g}", [128, T]) for g in range(2)]
    nr = [sb(f"nr{g}", [128, T]) for g in range(2)]
    ni = [sb(f"ni{g}", [128, T]) for g in range(2)]
    tmp = sb("tmpL", [128, T])
    nu_sb = sb("nu_sb", [128, 2])
    th_sb = sb("th_sb", [128, 2])
    gm_sb = sb("gm_sb", [128, 2])
    s1 = sb("sc1", [128, 2])
    s2 = sb("sc2", [128, 2])
    s3 = sb("sc3", [128, 2])
    lre = sb("lre", [128, 2])
    lim = sb("lim", [128, 2])
    pre = sb("pre", [128, 24])
    pim = sb("pim", [128, 24])
    pnim = sb("pnim", [128, 24])

    sems = [nc.semaphore(f"smL{i}").__enter__() for i in range(2)]
    block = nc.Block().__enter__()
    p = Prog()
    k = K(p)
    k.dma(nu_sb[:], nu_d)
    k.dma(th_sb[:], th_d)
    k.dma(gm_sb[:], gm_d)
    for g in range(2):
        k.dma(cr[g][:], br_d[128 * g:128 * g + 128, :])
        k.dma(ci[g][:], bi_d[128 * g:128 * g + 128, :])

    for g in range(2):
        gg = slice(g, g + 1)
        k.act(s1[:, gg], nu_sb[:, gg], AF.Exp)
        k.act(s1[:, gg], s1[:, gg], AF.Exp, scale=-1.0)
        k.act(s2[:, gg], th_sb[:, gg], AF.Exp)
        k.act(s3[:, gg], gm_sb[:, gg], AF.Exp)
        k.ts(cr[g][:], cr[g][:], s3[:, gg], None, ALU.mult)
        k.ts(ci[g][:], ci[g][:], s3[:, gg], None, ALU.mult)
        k.p.op("vector", lambda e, o=s3[:, gg], a=s2[:, gg]:
               e.tensor_single_scalar(o, a, LRU_PI, ALU.is_gt))
        k.stt(s2[:, gg], s3[:, gg], -2.0 * LRU_PI, s2[:, gg], ALU.mult, ALU.add)
        k.act(s3[:, gg], s2[:, gg], AF.Sin)
        k.tt(lim[:, gg], s1[:, gg], s3[:, gg], ALU.mult)
        k.ts(s2[:, gg], s2[:, gg], 0.5 * LRU_PI, None, ALU.add)
        k.p.op("vector", lambda e, o=s3[:, gg], a=s2[:, gg]:
               e.tensor_single_scalar(o, a, LRU_PI, ALU.is_gt))
        k.stt(s2[:, gg], s3[:, gg], -2.0 * LRU_PI, s2[:, gg], ALU.mult, ALU.add)
        k.act(s3[:, gg], s2[:, gg], AF.Sin)
        k.tt(lre[:, gg], s1[:, gg], s3[:, gg], ALU.mult)
        k.copy(pre[:, 12 * g:12 * g + 1], lre[:, gg])
        k.copy(pim[:, 12 * g:12 * g + 1], lim[:, gg])
        for j in range(1, 12):
            a = slice(12 * g + j - 1, 12 * g + j)
            o = slice(12 * g + j, 12 * g + j + 1)
            k.tt(s1[:, gg], pre[:, a], pre[:, a], ALU.mult)
            k.tt(s2[:, gg], pim[:, a], pim[:, a], ALU.mult)
            k.tt(pre[:, o], s1[:, gg], s2[:, gg], ALU.subtract)
            k.tt(s1[:, gg], pre[:, a], pim[:, a], ALU.mult)
            k.ts(pim[:, o], s1[:, gg], 2.0, None, ALU.mult)
        for j in range(12):
            o = slice(12 * g + j, 12 * g + j + 1)
            k.ts(pnim[:, o], pim[:, o], -1.0, None, ALU.mult)

    curr, curi, nxtr, nxti = cr, ci, nr, ni
    for j in range(12):
        d = 1 << j
        for g in range(2):
            pr = pre[:, 12 * g + j:12 * g + j + 1]
            pi = pim[:, 12 * g + j:12 * g + j + 1]
            pn = pnim[:, 12 * g + j:12 * g + j + 1]
            k.ts(tmp[:, 0:T - d], curr[g][:, 0:T - d], pr, None, ALU.mult)
            k.stt(tmp[:, 0:T - d], curi[g][:, 0:T - d], pn, tmp[:, 0:T - d],
                  ALU.mult, ALU.add)
            k.tt(nxtr[g][:, d:T], curr[g][:, d:T], tmp[:, 0:T - d], ALU.add)
            k.copy(nxtr[g][:, 0:d], curr[g][:, 0:d])
            k.ts(tmp[:, 0:T - d], curi[g][:, 0:T - d], pr, None, ALU.mult)
            k.stt(tmp[:, 0:T - d], curr[g][:, 0:T - d], pi, tmp[:, 0:T - d],
                  ALU.mult, ALU.add)
            k.tt(nxti[g][:, d:T], curi[g][:, d:T], tmp[:, 0:T - d], ALU.add)
            k.copy(nxti[g][:, 0:d], curi[g][:, 0:d])
        curr, curi, nxtr, nxti = nxtr, nxti, curr, curi

    for g in range(2):
        k.dma(xr_d[128 * g:128 * g + 128, :], curr[g][:])
        k.dma(xi_d[128 * g:128 * g + 128, :], curi[g][:])

    p.finalize(nc, block, sems, serial=True)
    block.__exit__(None, None, None)
    return nc


def run_lru(inp, st):
    nc = cached("lru", build_lru)
    Bur = {b: np.concatenate([st["bur"][2 * b], st["bur"][2 * b + 1]], axis=1) for b in range(B)}
    Bui = {b: np.concatenate([st["bui"][2 * b], st["bui"][2 * b + 1]], axis=1) for b in range(B)}
    in_maps = []
    for c in range(NC):
        b, nh = c // 2, c % 2
        rows = slice(256 * nh, 256 * nh + 256)
        m = dict(
            bur_u=np.ascontiguousarray(Bur[b][rows]),
            bui_u=np.ascontiguousarray(Bui[b][rows]),
            nu=np.ascontiguousarray(inp["L1_attn_nu_log"][rows].reshape(2, 128).T),
            th=np.ascontiguousarray(inp["L1_attn_theta_log"][rows].reshape(2, 128).T),
            gm=np.ascontiguousarray(inp["L1_attn_gamma_log"][rows].reshape(2, 128).T),
        )
        in_maps.append(m)
    res = run_bass_kernel_spmd(nc, in_maps, core_ids=list(range(NC)))
    Xr = {b: np.zeros((512, T), f32) for b in range(B)}
    Xi = {b: np.zeros((512, T), f32) for b in range(B)}
    for c in range(NC):
        b, nh = c // 2, c % 2
        Xr[b][256 * nh:256 * nh + 256] = res.results[c]["xre_u"]
        Xi[b][256 * nh:256 * nh + 256] = res.results[c]["xim_u"]
    st["xre"] = [np.ascontiguousarray(Xr[c // 2][:, (c % 2) * TC:(c % 2 + 1) * TC]) for c in range(NC)]
    st["xim"] = [np.ascontiguousarray(Xi[c // 2][:, (c % 2) * TC:(c % 2 + 1) * TC]) for c in range(NC)]
    return res



def build_l1c():
    nc = bass.Bass("TRN2", target_bir_lowering=False)
    dt = lambda n, s, k="ExternalInput": nc.dram_tensor(n, s, F32, kind=k).ap()
    xr_d = dt("xre", [512, TC])
    xi_d = dt("xim", [512, TC])
    du_d = dt("du", [D, TC])
    x_d = dt("x_fm", [D, TC])
    cr_d = dt("crt", [512, D])
    ci_d = dt("cit", [512, D])
    xo_d = dt("xo", [D, TC], "ExternalOutput")
    xl_d = dt("xln", [D, TC], "ExternalOutput")

    ctx = []

    def sb(name, shape):
        t = nc.sbuf_tensor(name, shape, F32)
        ctx.append(t)
        return t.__enter__()

    def ps(name, shape=(128, 512)):
        t = nc.psum_tensor(name, list(shape), F32)
        ctx.append(t)
        return t.__enter__()

    xrt = [sb(f"xrt{g}", [128, TC]) for g in range(4)]
    xit = [sb(f"xit{g}", [128, TC]) for g in range(4)]
    xg = [sb(f"xgc{d}", [128, 512]) for d in range(8)]
    dut = [sb(f"dut{d}", [128, 512]) for d in range(8)]
    yt = [sb(f"ytc{d}", [128, 512]) for d in range(8)]
    wstrip = sb("wstripc", [128, 1024])
    ln = LNScratch(nc)
    psS, psQ, psM, psR, ps4 = (ps(n) for n in ["psS", "psQ", "psM", "psR", "ps4"])
    sems = [nc.semaphore(f"sm{i}").__enter__() for i in range(4)]
    block = nc.Block().__enter__()
    p = Prog()
    k = K(p)
    ln.init(k)
    for g in range(4):
        k.dma(xrt[g][:], xr_d[128 * g:128 * g + 128, :])
        k.dma(xit[g][:], xi_d[128 * g:128 * g + 128, :])
        k.ts(xit[g][:], xit[g][:], -1.0, None, ALU.mult)
    for g in range(4):
        c0 = 512 * g
        for d in range(8):
            k.dma(dut[d][:], du_d[128 * d:128 * d + 128, c0:c0 + 512])
            k.dma(xg[d][:], x_d[128 * d:128 * d + 128, c0:c0 + 512])
        for oc in range(8):
            k.dma(wstrip[:, 0:512].rearrange("p (a o) -> p a o", a=4),
                  cr_d[:, 128 * oc:128 * oc + 128].rearrange("(a p) o -> p a o", p=128))
            k.dma(wstrip[:, 512:1024].rearrange("p (a o) -> p a o", a=4),
                  ci_d[:, 128 * oc:128 * oc + 128].rearrange("(a p) o -> p a o", p=128))
            for n in range(4):
                k.mm(ps4[:, 0:512], wstrip[:, 128 * n:128 * n + 128],
                     xrt[n][:, c0:c0 + 512], start=(n == 0), stop=False)
            for n in range(4):
                k.mm(ps4[:, 0:512], wstrip[:, 512 + 128 * n:512 + 128 * n + 128],
                     xit[n][:, c0:c0 + 512], start=False, stop=(n == 3))
            k.tt(yt[oc][:], ps4[:, 0:512], dut[oc][:], ALU.add)
        emit_ln(k, ln, yt, 0, 512, 8, psS, psQ, psM, psR)
        for d in range(8):
            k.tt(yt[d][:], yt[d][:], xg[d][:], ALU.add)
            k.dma(xo_d[128 * d:128 * d + 128, c0:c0 + 512], yt[d][:])
        emit_ln(k, ln, yt, 0, 512, 8, psS, psQ, psM, psR)
        for d in range(8):
            k.dma(xl_d[128 * d:128 * d + 128, c0:c0 + 512], yt[d][:])
    p.finalize(nc, block, sems)
    block.__exit__(None, None, None)
    return nc


def run_l1c(inp, st):
    nc = cached("l1c", build_l1c)
    shared = dict(
        crt=np.ascontiguousarray(inp["L1_attn_C_re"].T),
        cit=np.ascontiguousarray(inp["L1_attn_C_im"].T),
    )
    in_maps = []
    for c in range(NC):
        m = dict(shared)
        m["xre"] = st["xre"][c]
        m["xim"] = st["xim"][c]
        m["du"] = st["du"][c]
        m["x_fm"] = st["x_fm"][c]
        in_maps.append(m)
    res = run_bass_kernel_spmd(nc, in_maps, core_ids=list(range(NC)))
    st["x_fm"] = {c: res.results[c]["xo"] for c in range(NC)}
    st["xtf"] = {c: res.results[c]["xln"] for c in range(NC)}
    return res




XE = 2561
NTOK = 2560
ATT = 2176
KVW = 2432
DIP = 2816


def build_l1f():
    nc = bass.Bass("TRN2", target_bir_lowering=False)
    dt = lambda n, s, k="ExternalInput": nc.dram_tensor(n, s, F32, kind=k).ap()
    x_d = dt("x_ext", [D, NTOK])
    wqkv_d = dt("wqkv", [D, 3072])
    wo_d = dt("wo", [D, D])
    qs_d = dt("qs2", [128, 1])
    ks_d = dt("ks2", [128, 1])
    w1_d = dt("w1p", [D, 2 * DIP])
    b1_d = dt("b1p", [128, 44])
    w2_d = dt("w2p", [DIP, D])
    b2_d = dt("b2c", [128, 8])
    lng_d = dt("lng", [128, 8])
    lnb_d = dt("lnb", [128, 8])
    cs_d = dt("cs_t", [128, NTOK])
    ss_d = dt("ss_t", [128, NTOK])
    ci_d = dt("ci_t", [128, NTOK])
    si_d = dt("si_t", [128, NTOK])
    bh_d = dt("bh16", [8, 128, 16])
    bhT_d = dt("bh16t", [8, 16, 128])
    rot_d = dt("rotm", [128, 128])
    msk_d = dt("masks", [17, 128, 384])
    id_d = dt("ident", [128, 128])
    xo_d = dt("xo", [D, TC], "ExternalOutput")
    q_s = dt("q_scr", [D, NTOK], "Internal")
    k_s = dt("k_scr", [D, NTOK], "Internal")
    v_s = dt("v_scr", [D, NTOK], "Internal")
    a_s = dt("att_scr", [D, ATT], "Internal")
    xm_s = dt("xm_scr", [D, ATT], "Internal")

    allocs = []

    def sb(name, shape, scoped=None):
        t = nc.sbuf_tensor(name, shape, F32)
        h = t.__enter__()
        (scoped if scoped is not None else allocs).append(t)
        return h

    def ps(name, shape=(128, 512)):
        t = nc.psum_tensor(name, list(shape), F32)
        allocs.append(t)
        return t.__enter__()

    ident = sb("identf", [128, 128])
    rot = sb("rotf", [128, 128])
    bh = sb("bhf", [128, 8 * 16])
    bhT = sb("bhTf", [16, 8 * 128])
    qs2 = sb("qs2f", [128, 1])
    ks2 = sb("ks2f", [128, 1])
    b1p = sb("b1pf", [128, 44])
    b2c = sb("b2cf", [128, 8])
    lng = sb("lngf", [128, 8])
    lnb = sb("lnbf", [128, 8])
    ln = LNScratch(nc)
    psS, psQ, psM, psR = (ps(n) for n in ["psS", "psQ", "psM", "psR"])
    psA = ps("psA")
    psB = ps("psB")
    psC = ps("psC")
    psD = ps("psD")

    sems = [nc.semaphore(f"sm{i}").__enter__() for i in range(8)]
    block = nc.Block().__enter__()
    p = Prog()
    k = K(p)
    ln.init(k)
    k.dma(ident[:], id_d)
    k.dma(rot[:], rot_d)
    k.dma(bh[:].rearrange("p (a o) -> p a o", a=8), bh_d.rearrange("a p o -> p a o"))
    k.dma(bhT[:].rearrange("p (a o) -> p a o", a=8), bhT_d.rearrange("a p o -> p a o"))
    k.dma(qs2[:], qs_d)
    k.dma(ks2[:], ks_d)
    k.dma(b1p[:], b1_d)
    k.dma(b2c[:], b2_d)
    k.dma(lng[:], lng_d)
    k.dma(lnb[:], lnb_d)

    sc1 = []
    xe = [sb(f"xe{d}", [128, XE], sc1) for d in range(8)]
    tabs = {n: sb(f"tab_{n}", [128, 512], sc1) for n in ["cs", "ss", "ci", "si"]}
    qg = [sb(f"qg{d}", [128, 512], sc1) for d in range(8)]
    kg = [sb(f"kg{d}", [128, 512], sc1) for d in range(8)]
    wstrip = sb("wstrip1", [128, 1024], sc1)
    qp = sb("qp1", [128, 512], sc1)
    t1 = sb("t1f", [128, 512], sc1)
    t2 = sb("t2f", [128, 512], sc1)
    sqg = sb("sqg", [128, 512], sc1)
    s16 = sb("s16f", [16, 512], sc1)
    si16 = sb("si16f", [16, 512], sc1)

    for d in range(8):
        k.memset(xe[d][:, 0:1], 0.0)
        k.dma(xe[d][:, 1:XE], x_d[128 * d:128 * d + 128, :])

    for g in range(5):
        j0 = 512 * g
        c0 = j0 + 1
        xsl = [xe[d][:, c0:c0 + 512] if d < 4 else xe[d][:, c0 - 1:c0 - 1 + 512]
               for d in range(8)]
        for oc in range(24):
            k.dma(wstrip[:].rearrange("p (a o) -> p a o", a=8),
                  wqkv_d[:, 128 * oc:128 * oc + 128].rearrange("(a p) o -> p a o", p=128))
            for d in range(8):
                k.mm(psA[:, 0:512], wstrip[:, 128 * d:128 * d + 128], xsl[d],
                     start=(d == 0), stop=(d == 7))
            if oc < 8:
                k.copy(qg[oc][:], psA[:, 0:512])
            elif oc < 16:
                k.copy(kg[oc - 8][:], psA[:, 0:512])
            else:
                k.copy(qp[:], psA[:, 0:512])
                k.dma(v_s[128 * (oc - 16):128 * (oc - 16) + 128, j0:j0 + 512], qp[:])
        for n in ["cs", "ss", "ci", "si"]:
            dd = {"cs": cs_d, "ss": ss_d, "ci": ci_d, "si": si_d}[n]
            k.dma(tabs[n][:], dd[:, j0:j0 + 512])
        for (gt, scv, tA, tB, dst) in [(qg, qs2, "cs", "ss", q_s),
                                       (kg, ks2, "ci", "si", k_s)]:
            for d in range(8):
                k.act(sqg[:], gt[d][:], AF.Square)
                k.mm(psB[0:16, 0:512], bh[:, 16 * d:16 * d + 16], sqg[:],
                     start=(d == 0), stop=(d == 7))
            k.act(s16[:], psB[0:16, 0:512], AF.Sqrt)
            k.ts(s16[:], s16[:], 1e-12, None, ALU.max)
            k.recip(si16[:], s16[:])
            for d in range(8):
                k.mm(psC[:, 0:512], bhT[:, 128 * d:128 * d + 128], si16[:])
                k.stt(qp[:], gt[d][:], scv[:], psC[:, 0:512], ALU.mult, ALU.mult)
                k.mm(psD[:, 0:512], rot[:], qp[:])
                k.tt(t1[:], qp[:], tabs[tA][:], ALU.mult)
                k.tt(t2[:], psD[:, 0:512], tabs[tB][:], ALU.mult)
                k.tt(qp[:], t1[:], t2[:], ALU.add)
                k.dma(dst[128 * d:128 * d + 128, j0:j0 + 512], qp[:])
    for t in reversed(sc1):
        t.__exit__(None, None, None)

    sc2 = []
    q_sb = sb("q_sb", [64, ATT], sc2)
    k_sb = sb("k_sb", [64, KVW], sc2)
    v_sb = sb("v_sb", [64, KVW], sc2)
    vT = [sb(f"vT{j}", [128, 64], sc2) for j in range(19)]
    att_sb = sb("att_sb", [64, ATT], sc2)
    msk = sb("mskf", [128, 17 * 384], sc2)
    smt = sb("smt", [128, 384], sc2)
    Et = sb("Et", [128, 384], sc2)
    At = sb("At", [128, 384], sc2)
    ATt = sb("ATt", [128, 384], sc2)
    mx1 = sb("mx1", [128, 1], sc2)
    ri1 = sb("ri1", [128, 1], sc2)

    k.dma(msk[:].rearrange("p (a o) -> p a o", a=17), msk_d.rearrange("a p o -> p a o"))
    for h in range(16):
        rq = slice(64 * h, 64 * h + 64)
        k.dma(q_sb[:], q_s[rq, 384:NTOK])
        k.dma(k_sb[:], k_s[rq, 128:NTOK])
        k.dma(v_sb[:], v_s[rq, 128:NTOK])
        for j in range(19):
            k.tr(psD[0:128, 0:64], v_sb[:, 128 * j:128 * j + 128], ident[0:64, 0:64])
            k.copy(vT[j][:], psD[0:128, 0:64])
        for ic in range(17):
            k.mm(psA[:, 0:384], q_sb[:, 128 * ic:128 * ic + 128],
                 k_sb[:, 128 * ic:128 * ic + 384])
            k.stt(smt[:], psA[:, 0:384], 8.0, msk[:, 384 * ic:384 * ic + 384],
                  ALU.mult, ALU.add)
            k.rmax(mx1[:], smt[:], negate=True)
            k.act(Et[:], smt[:], AF.Exp, bias=mx1[:])
            k.rsum(ri1[:], Et[:])
            k.recip(ri1[:], ri1[:])
            k.ts(At[:], Et[:], ri1[:], None, ALU.mult)
            for j in range(3):
                k.tr(psB[0:128, 0:128], At[:, 128 * j:128 * j + 128], ident[:])
                k.copy(ATt[:, 128 * j:128 * j + 128], psB[0:128, 0:128])
            for j in range(3):
                k.mm(psC[0:64, 0:128], vT[ic + j][:], ATt[:, 128 * j:128 * j + 128],
                     start=(j == 0), stop=(j == 2))
            k.copy(att_sb[:, 128 * ic:128 * ic + 128], psC[0:64, 0:128])
        k.dma(a_s[rq, :], att_sb[:])
    for t in reversed(sc2):
        t.__exit__(None, None, None)

    sc3 = []
    at3 = [sb(f"at3{d}", [128, 512], sc3) for d in range(8)]
    xg3 = [sb(f"xg3{d}", [128, 512], sc3) for d in range(8)]
    ws3 = sb("ws3", [128, 1024], sc3)
    yt3 = sb("yt3", [128, 512], sc3)
    for (a0, w) in [(0, 128), (128, 512), (640, 512), (1152, 512), (1664, 512)]:
        for d in range(8):
            k.dma(at3[d][:, 0:w], a_s[128 * d:128 * d + 128, a0:a0 + w])
            k.dma(xg3[d][:, 0:w], x_d[128 * d:128 * d + 128, a0 + 384:a0 + 384 + w])
        for oc in range(8):
            k.dma(ws3[:].rearrange("p (a o) -> p a o", a=8),
                  wo_d[:, 128 * oc:128 * oc + 128].rearrange("(a p) o -> p a o", p=128))
            for d in range(8):
                k.mm(psA[:, 0:w], ws3[:, 128 * d:128 * d + 128], at3[d][:, 0:w],
                     start=(d == 0), stop=(d == 7))
            k.tt(yt3[:, 0:w], psA[:, 0:w], xg3[oc][:, 0:w], ALU.add)
            k.dma(xm_s[128 * oc:128 * oc + 128, a0:a0 + w], yt3[:, 0:w])
    for t in reversed(sc3):
        t.__exit__(None, None, None)

    sc4 = []
    xm4 = [sb(f"xm4{d}", [128, 513], sc4) for d in range(8)]
    xn4 = [sb(f"xn4{d}", [128, 512], sc4) for d in range(8)]
    ht4 = [sb(f"ht4{j}", [128, 512], sc4) for j in range(22)]
    gt4 = sb("gt4", [64, 512], sc4)
    ws4 = sb("ws4", [128, 1024], sc4)
    ws4b = sb("ws4b", [128, DIP], sc4)
    yt4 = sb("yt4", [128, 512], sc4)
    for tg in range(4):
        c0 = 512 * tg + 128
        for d in range(8):
            k.dma(xm4[d][:], xm_s[128 * d:128 * d + 128, c0 - 1:c0 + 512])
        ins = [xm4[d][:, 1:513] if d < 4 else xm4[d][:, 0:512] for d in range(8)]
        emit_ln_aps(k, ln, ins, [xn4[d][:] for d in range(8)], 512,
                    psS, psQ, psM, psR, affine=(lng, lnb, 0))
        for oc in range(44):
            k.dma(ws4[:].rearrange("p (a o) -> p a o", a=8),
                  w1_d[:, 128 * oc:128 * oc + 128].rearrange("(a p) o -> p a o", p=128))
            for d in range(8):
                k.mm(psA[:, 0:512], ws4[:, 128 * d:128 * d + 128], xn4[d][:],
                     start=(d == 0), stop=(d == 7))
            k.act(gt4[:], psA[64:128, 0:512], AF.Gelu, bias=b1p[64:128, oc:oc + 1])
            j, hf = oc // 2, 64 * (oc % 2)
            k.stt(ht4[j][hf:hf + 64, :], psA[0:64, 0:512], b1p[0:64, oc:oc + 1],
                  gt4[:], ALU.add, ALU.mult)
        for oc in range(8):
            for q in range(3):
                r0 = 1024 * q
                nr = min(1024, DIP - r0)
                k.dma(ws4b[:, r0:r0 + nr].rearrange("p (a o) -> p a o", o=128),
                      w2_d[r0:r0 + nr, 128 * oc:128 * oc + 128]
                      .rearrange("(a p) o -> p a o", p=128))
            for j in range(22):
                k.mm(psB[:, 0:512], ws4b[:, 128 * j:128 * j + 128], ht4[j][:],
                     start=(j == 0), stop=(j == 21))
            k.stt(yt4[:], psB[:, 0:512], b2c[:, oc:oc + 1], xm4[oc][:, 1:513],
                  ALU.add, ALU.add)
            k.dma(xo_d[128 * oc:128 * oc + 128, 512 * tg:512 * tg + 512], yt4[:])
    for t in reversed(sc4):
        t.__exit__(None, None, None)

    p.finalize(nc, block, sems)
    block.__exit__(None, None, None)
    return nc


def _l1f_consts():
    out = {}
    half_dims = np.arange(0, 64, 2, dtype=np.float64)
    inv_freq = 1.0 / 10000.0 ** (half_dims / 64.0)
    base_scale = (half_dims + 0.4 * 64) / (1.4 * 64)
    for hf in range(2):
        tok = np.arange(-512, 2048, dtype=np.float64) + 2048 * hf
        freqs = tok[:, None] * inv_freq[None, :]
        freqs = np.concatenate([freqs, freqs], -1)
        power = (tok - 2048.0) / 128.0
        scale = base_scale[None, :] ** power[:, None]
        scale = np.concatenate([scale, scale], -1)
        cs = (np.cos(freqs) * scale).astype(f32)
        ss = (np.sin(freqs) * scale).astype(f32)
        civ = (np.cos(freqs) / scale).astype(f32)
        siv = (np.sin(freqs) / scale).astype(f32)
        stack2 = lambda a: np.ascontiguousarray(np.concatenate([a.T, a.T], 0))
        out[hf] = dict(cs_t=stack2(cs), ss_t=stack2(ss),
                       ci_t=stack2(civ), si_t=stack2(siv))
        iq = np.arange(128)[:, None]
        jc = np.arange(384)[None, :]
        masks = np.zeros((17, 128, 384), f32)
        for ic in range(17):
            allowed = (jc >= iq) & (jc <= iq + 256)
            if hf == 0:
                allowed = allowed & ((128 * ic - 384 + jc) >= 0)
            masks[ic] = np.where(allowed, 0.0, NEG)
        out[hf]["masks"] = masks
    bh = np.zeros((8, 128, 16), f32)
    for d in range(8):
        bh[d, 0:64, 2 * d] = 1.0
        bh[d, 64:128, 2 * d + 1] = 1.0
    rotm = np.zeros((128, 128), f32)
    for m in range(128):
        mm_ = m % 64
        base = m - mm_
        if mm_ < 32:
            rotm[base + mm_ + 32, m] = -1.0
        else:
            rotm[base + mm_ - 32, m] = 1.0
    out["bh16"] = bh
    out["bh16t"] = np.ascontiguousarray(bh.transpose(0, 2, 1))
    out["rotm"] = rotm
    out["ident"] = np.eye(128, dtype=f32)
    return out


def _l1f_weights(inp, i):
    DI = 2730
    W1 = inp["L1_ffn_W1"][i]
    b1 = inp["L1_ffn_b1"][i]
    W1a = np.zeros((D, DIP), f32); W1a[:, :DI] = W1[:, :DI]
    W1g = np.zeros((D, DIP), f32); W1g[:, :DI] = W1[:, DI:]
    b1a = np.zeros(DIP, f32); b1a[:DI] = b1[:DI]
    b1g = np.zeros(DIP, f32); b1g[:DI] = b1[DI:]
    W1p = np.zeros((D, 2 * DIP), f32)
    b1p = np.zeros((128, 44), f32)
    for kk in range(44):
        W1p[:, 128 * kk:128 * kk + 64] = W1a[:, 64 * kk:64 * kk + 64]
        W1p[:, 128 * kk + 64:128 * kk + 128] = W1g[:, 64 * kk:64 * kk + 64]
        b1p[0:64, kk] = b1a[64 * kk:64 * kk + 64]
        b1p[64:128, kk] = b1g[64 * kk:64 * kk + 64]
    W2p = np.zeros((DIP, D), f32)
    W2p[:DI] = inp["L1_ffn_W2"][i]
    return dict(
        wqkv=np.ascontiguousarray(inp["L1_ffn_Wqkv"][i]),
        wo=np.ascontiguousarray(inp["L1_ffn_Wo"][i]),
        qs2=np.ascontiguousarray(np.tile(inp["L1_ffn_q_scale"][i], 2).reshape(128, 1)),
        ks2=np.ascontiguousarray(np.tile(inp["L1_ffn_k_scale"][i], 2).reshape(128, 1)),
        w1p=W1p, b1p=b1p, w2p=W2p,
        b2c=np.ascontiguousarray(inp["L1_ffn_b2"][i].reshape(8, 128).T),
        lng=np.ascontiguousarray(inp["L1_ffn_ln_g"][i].reshape(8, 128).T),
        lnb=np.ascontiguousarray(inp["L1_ffn_ln_b"][i].reshape(8, 128).T),
    )


def run_l1f_layer(inp, st, i, consts):
    nc = cached("l1f", build_l1f)
    wts = _l1f_weights(inp, i)
    in_maps = []
    for c in range(NC):
        b, hf = c // 2, c % 2
        m = dict(wts)
        m.update({kk: consts[hf][kk] for kk in ["cs_t", "ss_t", "ci_t", "si_t", "masks"]})
        m["bh16"] = consts["bh16"]
        m["bh16t"] = consts["bh16t"]
        m["rotm"] = consts["rotm"]
        m["ident"] = consts["ident"]
        if hf == 0:
            halo = np.zeros((D, 512), f32)
        else:
            halo = st["xtf"][c - 1][:, TC - 512:TC]
        m["x_ext"] = np.ascontiguousarray(np.concatenate([halo, st["xtf"][c]], axis=1))
        in_maps.append(m)
    res = run_bass_kernel_spmd(nc, in_maps, core_ids=list(range(NC)))
    st["xtf"] = {c: res.results[c]["xo"] for c in range(NC)}
    return res



def build_fin():
    nc = bass.Bass("TRN2", target_bir_lowering=False)
    dt = lambda n, s, k="ExternalInput": nc.dram_tensor(n, s, F32, kind=k).ap()
    y_d = dt("y_fm", [D, TC])
    x_d = dt("x_fm", [D, TC])
    xo_d = dt("xo", [D, TC], "ExternalOutput")
    allocs = []

    def sb(name, shape):
        t = nc.sbuf_tensor(name, shape, F32)
        allocs.append(t)
        return t.__enter__()

    def ps(name, shape=(128, 512)):
        t = nc.psum_tensor(name, list(shape), F32)
        allocs.append(t)
        return t.__enter__()

    yt = [sb(f"yt{d}", [128, 512]) for d in range(8)]
    xt = [sb(f"xt{d}", [128, 512]) for d in range(8)]
    ln = LNScratch(nc)
    psS, psQ, psM, psR = (ps(n) for n in ["psS", "psQ", "psM", "psR"])
    sems = [nc.semaphore(f"sm{i}").__enter__() for i in range(2)]
    block = nc.Block().__enter__()
    p = Prog()
    k = K(p)
    ln.init(k)
    for g in range(4):
        c0 = 512 * g
        for d in range(8):
            k.dma(yt[d][:], y_d[128 * d:128 * d + 128, c0:c0 + 512])
            k.dma(xt[d][:], x_d[128 * d:128 * d + 128, c0:c0 + 512])
        emit_ln(k, ln, yt, 0, 512, 8, psS, psQ, psM, psR)
        for d in range(8):
            k.tt(yt[d][:], yt[d][:], xt[d][:], ALU.add)
            k.dma(xo_d[128 * d:128 * d + 128, c0:c0 + 512], yt[d][:])
    p.finalize(nc, block, sems)
    block.__exit__(None, None, None)
    return nc


def run_fin(inp, st):
    nc = cached("fin", build_fin)
    in_maps = [dict(y_fm=st["xtf"][c], x_fm=st["x_fm"][c]) for c in range(NC)]
    res = run_bass_kernel_spmd(nc, in_maps, core_ids=list(range(NC)))
    st["x_fm"] = {c: res.results[c]["xo"] for c in range(NC)}
    return res



def kernel(**inp):
    inp = {k: np.asarray(v, dtype=np.float32) for k, v in inp.items()}
    st = {}
    run_l0a(inp, st)
    run_rwkv(inp, st)
    run_l0b(inp, st)
    run_l0ffn(inp, st)
    run_l1a(inp, st)
    run_lru(inp, st)
    run_l1c(inp, st)
    consts = _l1f_consts()
    for i in range(6):
        run_l1f_layer(inp, st, i, consts)
    run_fin(inp, st)
    out = np.zeros((B, T, D), np.float32)
    for c in range(NC):
        b, hf = c // 2, c % 2
        out[b, hf * TC:(hf + 1) * TC, :] = st["x_fm"][c].T
    return out
